# version 07
# baseline (speedup 1.0000x reference)
import contextlib
import numpy as np
import ml_dtypes
import concourse.bass as bass
import concourse.mybir as mybir
from concourse.bass_utils import run_bass_kernel_spmd

F32 = mybir.dt.float32
BF16 = mybir.dt.bfloat16
AF = mybir.ActivationFunctionType
ALU = mybir.AluOpType
AX = mybir.AxisListType
BF = ml_dtypes.bfloat16

D = 4096
T = 1024
C = 32
H = 32
S = 4096
NR = 4
ER = 64
KVH = 131072
KV1 = 2 * H * KVH + 128
RSQ = 0.08838834764831845
EPS = 1e-5


class Sem:
    def __init__(self, es, nc, name):
        self.h = es.enter_context(nc.semaphore(name))
        self.c = 0


def _inc(instr, s, k):
    instr.then_inc(s.h, k)
    s.c += k


def _w(eng, s, v):
    assert v <= s.c, f"wait ahead of emission: {v} > {s.c}"
    if v > 0:
        eng.wait_ge(s.h, v)


def _wm(eng, mark):
    sem, val = mark
    _w(eng, sem, val)


class _Stop(Exception):
    pass


def build(stop=None, dbg=False):
    nc = bass.Bass("TRN2", target_bir_lowering=False, detect_race_conditions=False)
    es = contextlib.ExitStack()
    try:
        _build_body(nc, es, stop, dbg)
    except _Stop:
        pass
    return nc


def _build_body(nc, es, stop, dbg=False):
    x_in = nc.dram_tensor("x_fm", [C, 128, T], F32, kind="ExternalInput")
    wq = nc.dram_tensor("wq_blk", [C, 128, D], F32, kind="ExternalInput")
    wk = nc.dram_tensor("wk_blk", [C, 128, D], F32, kind="ExternalInput")
    wv = nc.dram_tensor("wv_blk", [C, 128, D], F32, kind="ExternalInput")
    wo = nc.dram_tensor("wo_blk", [C, 128, D], F32, kind="ExternalInput")
    a_blk = nc.dram_tensor("a_blk", [128, C * ER], F32, kind="ExternalInput")
    b_e = nc.dram_tensor("b_e", [ER, D], F32, kind="ExternalInput")
    wr_blk = nc.dram_tensor("wr_blk", [128, 128], F32, kind="ExternalInput")
    e4 = nc.dram_tensor("e4", [4, ER], F32, kind="ExternalInput")
    idb_in = nc.dram_tensor("idb", [128, 128], BF16, kind="ExternalInput")
    idf_in = nc.dram_tensor("idf", [128, 128], F32, kind="ExternalInput")
    onesb_in = nc.dram_tensor("onesb", [128, 1], BF16, kind="ExternalInput")
    onesf_in = nc.dram_tensor("onesf", [128, 1], F32, kind="ExternalInput")
    onesrow_in = nc.dram_tensor("onesrow", [1, 128], F32, kind="ExternalInput")

    y_out = nc.dram_tensor("y", [C, 128, T], F32, kind="ExternalOutput")

    _ik = dict(kind="ExternalOutput") if dbg else {}
    xr1 = nc.dram_tensor("xr1", [C, 128, T], F32, **_ik)
    xr2 = nc.dram_tensor("xr2", [C, 128, T], F32, **_ik)
    qd = nc.dram_tensor("q_dram", [H, 128, T], BF16)
    kv_in = nc.dram_tensor("kv_in", [KV1], BF16)
    kv_out = nc.dram_tensor("kv_out", [NR * KV1], BF16)

    with es:
        sb = lambda n, sh, dt: es.enter_context(nc.sbuf_tensor(n, sh, dt))
        psb = lambda n, sh, dt: es.enter_context(nc.psum_tensor(n, sh, dt))

        Hb = sb("Hb", [128, C * T], BF16)
        Wb = sb("Wb", [128, 2 * D], BF16)
        xst = sb("xst", [128, 2 * T], F32)
        rst = sb("rst", [128, 2 * T], F32)
        Mbc = sb("Mbc", [128, T], F32)
        Gsb = Mbc[0:ER, :]
        Rbc = sb("Rbc", [128, T], F32)
        xbb = sb("xbb", [128, T], BF16)
        xqq = sb("xqq", [128, T], BF16)
        Kh = sb("Kh", [128, 2 * S], BF16)
        Vh = sb("Vh", [128, 2 * S], BF16)
        Pt = sb("Pt", [128, 2 * S], BF16)
        PT = sb("PT", [128, S], BF16)
        qsqb = xbb
        qsqrow = sb("qsqrow", [1, T], F32)
        negmh = sb("negmh", [128, 8], F32)
        mh1 = sb("mh1", [128, 8], F32)
        lacc = sb("lacc", [128, 8], F32)
        lsb = sb("lsb", [128, 1], F32)
        linv = sb("linv", [128, 1], F32)
        linvrow = sb("linvrow", [1, 128], F32)
        bndall = linvrow
        linvbc = sb("linvbc", [128, 128], F32)
        brow = sb("brow", [1, H], F32)
        Bbc = sb("Bbc", [128, H], F32)
        Vfm = xqq
        Qst = sb("Qst", [128, 2 * T], BF16)
        Vtm = Qst
        Qh = Qst
        Kst = Pt
        kmax = sb("kmax", [128, 1], F32)
        bnd_bf = sb("bnd_bf", [1, 128], BF16)
        zsb = sb("zsb", [4, T], F32)
        gall = sb("gall", [128, C], F32)
        gfm = sb("gfm", [4, T], BF16)

        gmid = sb("gmid", [ER, T], BF16)
        sc1a = sb("sc1a", [128, 1], F32)
        sc1b = sb("sc1b", [128, 1], F32)
        sc1c = sb("sc1c", [128, 1], F32)
        sc4a = sb("sc4a", [128, 4], F32)
        sc4b = sb("sc4b", [128, 4], F32)
        sc4c = sb("sc4c", [128, 4], F32)
        sc4d = sb("sc4d", [128, 4], F32)
        murow = sb("murow", [1, T], F32)
        rrow = sb("rrow", [1, T], F32)
        rowT2 = sb("rowT2", [1, T], F32)
        rowT1 = qsqrow
        A_sb = sb("A_sb", [128, C * ER], BF16)
        B_sb = sb("B_sb", [ER, D], BF16)
        Wr_sb = sb("Wr_sb", [128, 128], BF16)
        E4_sb = sb("E4_sb", [4, ER], BF16)
        idb = sb("idb_sb", [128, 128], BF16)
        idf = sb("idf_sb", [128, 128], F32)
        onesb = sb("onesb_sb", [128, 1], BF16)
        onesf = sb("onesf_sb", [128, 1], F32)
        onesrow = sb("onesrow_sb", [1, 128], F32)

        psS = [psb(f"psq{i}", [128, T], F32) for i in range(2)]
        pst2 = lambda buf, j: psS[buf][:, 512 * j:512 * (j + 1)]
        pstat = psb("pstat", [128, T], F32)
        pptb = [psb(f"ppt{i}", [128, 512], BF16) for i in range(2)]
        psml = pstat

        sC = Sem(es, nc, "sC")
        sB = Sem(es, nc, "sB")
        sCC = Sem(es, nc, "sCC")
        sP = Sem(es, nc, "sP")
        sV = Sem(es, nc, "sV")
        sA = Sem(es, nc, "sA")
        sWp = [Sem(es, nc, f"sW{i}") for i in range(2)]
        sXp = [Sem(es, nc, f"sX{i}") for i in range(2)]
        sTp = [Sem(es, nc, f"sT{i}") for i in range(2)]
        sKVp = [Sem(es, nc, f"sKV{i}") for i in range(2)]
        sQDp = [Sem(es, nc, f"sQD{i}") for i in range(2)]
        sHp = [Sem(es, nc, f"sH{i}") for i in range(2)]

        block = es.enter_context(nc.Block())
        TE, VE, SE, SY, GP = nc.tensor, nc.vector, nc.scalar, nc.sync, nc.gpsimd

        def ckpt(tag):
            if stop == tag:
                for s in [*sTp, *sKVp, *sQDp, *sXp, *sHp]:
                    SY.wait_ge(s.h, s.c)
                for s in [*sWp, sC, sB]:
                    GP.wait_ge(s.h, s.c)
                raise _Stop()

        for dst, src in [(idb, idb_in), (idf, idf_in), (onesb, onesb_in),
                         (onesf, onesf_in), (onesrow, onesrow_in),
                         (A_sb, a_blk), (B_sb, b_e), (Wr_sb, wr_blk), (E4_sb, e4)]:
            _inc(GP.dma_start(out=dst[:, :], in_=src[:, :]), sC, 16)
        cst_w = sC.c
        ckpt('p0')

        def emit_stat_mms(c):
            for j in range(2):
                TE.matmul(pstat[0:1, 512 * j:512 * (j + 1)], onesb[:, :],
                          xbb[:, 512 * j:512 * (j + 1)], start=(c == 0), stop=(c == C - 1))
            for j in range(2):
                ins = TE.matmul(pstat[32:33, 512 * j:512 * (j + 1)], onesb[:, :],
                                xqq[:, 512 * j:512 * (j + 1)], start=(c == 0), stop=(c == C - 1))
                if j == 1:
                    _inc(ins, sP, 1)

        def ln_rows_and_bcast(stats_p_inc):
            _w(VE, sP, stats_p_inc)
            _inc(VE.tensor_scalar(murow[:, :], pstat[0:1, 0:T], 1.0 / D, None, ALU.mult), sV, 1)
            _w(VE, sV, sV.c)
            _inc(VE.tensor_mul(rowT1[:, :], murow[:, :], murow[:, :]), sV, 1)
            _w(VE, sV, sV.c)
            _inc(VE.scalar_tensor_tensor(rowT2[:, :], pstat[32:33, 0:T], 1.0 / D,
                                         rowT1[:, :], ALU.mult, ALU.subtract), sV, 1)
            _w(VE, sV, sV.c)
            _inc(VE.tensor_scalar(rowT1[:, :], rowT2[:, :], EPS, None, ALU.add), sV, 1)
            _w(SE, sV, sV.c)
            _inc(SE.activation(rowT2[:, :], rowT1[:, :], AF.Sqrt), sA, 1)
            _w(VE, sA, sA.c)
            _inc(VE.reciprocal(rrow[:, :], rowT2[:, :]), sV, 1)
            _w(TE, sV, sV.c)
            for j in range(2):
                TE.matmul(pst2(0, j), onesrow[:, :],
                          murow[0:1, 512 * j:512 * (j + 1)], start=True, stop=True)
            for j in range(2):
                ins = TE.matmul(pst2(1, j), onesrow[:, :],
                                rrow[0:1, 512 * j:512 * (j + 1)], start=True, stop=True)
                if j == 1:
                    _inc(ins, sP, 1)
            _w(VE, sP, sP.c)
            for j in range(2):
                VE.tensor_copy(Mbc[:, 512 * j:512 * (j + 1)], pst2(0, j))
            for j in range(2):
                ins = VE.tensor_copy(Rbc[:, 512 * j:512 * (j + 1)], pst2(1, j))
            _inc(ins, sV, 1)
            return sV.c

        ve_mark = [0] * C
        se_mark = [0] * C
        stat_mark = [0] * C
        for c in range(C):
            buf = c % 2
            sx = sXp[buf]
            if c >= 2:
                _w(SY, sV, ve_mark[c - 2])
                _w(SY, sA, se_mark[c - 2])
            _inc(SY.dma_start(out=xst[:, T * buf:T * (buf + 1)],
                              in_=x_in[c, :, :]), sx, 16)
            lm = (sx, sx.c)
            xv = xst[:, T * buf:T * (buf + 1)]
            _wm(VE, lm)
            if c >= 1:
                _w(VE, sP, stat_mark[c - 1])
            _inc(VE.tensor_copy(xbb[:, :], xv), sV, 1)
            ve_mark[c] = sV.c
            _wm(SE, lm)
            _w(SE, sC, cst_w)
            if c >= 1:
                _w(SE, sP, stat_mark[c - 1])
            _inc(SE.activation(xqq[:, :], xv, AF.Square), sA, 1)
            se_mark[c] = sA.c
            _w(TE, sV, ve_mark[c])
            _w(TE, sA, se_mark[c])
            _w(TE, sC, cst_w)
            emit_stat_mms(c)
            stat_mark[c] = sP.c
        ckpt('p1')
        ln_rows_and_bcast(stat_mark[C - 1])
        bc0 = sV.c
        ckpt('p1b')

        def h_pass(src_dram, bc_ready, with_ffn):
            h_done = [0] * C
            entry_v, entry_a, entry_p = sV.c, sA.c, sP.c
            entry_t = [sTp[0].c, sTp[1].c]
            for c in range(C):
                buf = c % 2
                sx = sXp[buf]
                if c >= 2:
                    _w(SY, sV, h_done[c - 2])
                else:
                    _w(SY, sV, entry_v)
                    _w(SY, sA, entry_a)
                _inc(SY.dma_start(out=xst[:, T * buf:T * (buf + 1)],
                                  in_=src_dram[c, :, :]), sx, 16)
                lm = (sx, sx.c)
                xv = xst[:, T * buf:T * (buf + 1)]
                hv = Hb[:, T * c:T * (c + 1)]
                _wm(VE, lm)
                if c == 0:
                    _w(VE, sV, bc_ready)
                    _w(VE, sP, entry_p)
                    _w(VE, sTp[0], entry_t[0])
                    _w(VE, sTp[1], entry_t[1])
                else:
                    _w(VE, sV, h_done[c - 1])
                _inc(VE.tensor_sub(rst[:, 0:T], xv, Mbc[:, :]), sV, 1)
                _w(VE, sV, sV.c)
                _inc(VE.tensor_mul(hv, rst[:, 0:T], Rbc[:, :]), sV, 1)
                h_done[c] = sV.c
                if with_ffn:
                    _w(TE, sV, h_done[c])
                    for j in range(2):
                        TE.matmul(psS[0][0:4, 512 * j:512 * (j + 1)], Wr_sb[:, 4 * c:4 * (c + 1)],
                                  hv[:, 512 * j:512 * (j + 1)],
                                  start=(c == 0), stop=(c == C - 1))
                    for j in range(2):
                        ins = TE.matmul(psS[1][0:ER, 512 * j:512 * (j + 1)],
                                        A_sb[:, ER * c:ER * (c + 1)],
                                        hv[:, 512 * j:512 * (j + 1)],
                                        start=(c == 0), stop=(c == C - 1))
                        if c == C - 1 and j == 1:
                            _inc(ins, sP, 1)
            return sP.c if with_ffn else sV.c

        zmid_p = h_pass(x_in, bc0, with_ffn=True)
        ckpt('p2')

        def gates_and_gmid(zmid_inc, tags=False):
            def gck(t):
                if tags:
                    ckpt(t)
            _w(VE, sP, zmid_inc)
            VE.tensor_copy(zsb[:, 0:512], psS[0][0:4, 0:512])
            _inc(VE.tensor_copy(zsb[:, 512:T], psS[0][0:4, 512:T]), sV, 1)
            zevac = sV.c
            gck('g1')
            _w(TE, sV, zevac)
            for t in range(8):
                ins = TE.transpose(psml[:, 4 * t:4 * (t + 1)],
                                   zsb[0:4, 128 * t:128 * (t + 1)], idf[0:4, 0:4])
                if t == 7:
                    _inc(ins, sP, 1)
            ztp = sP.c
            gck('g2')
            def VS(instr):
                _inc(instr, sV, 1)
                _w(VE, sV, sV.c)
            for t in range(8):
                zt = psml[:, 4 * t:4 * (t + 1)]
                if t == 0:
                    _w(VE, sP, ztp)
                VS(VE.reduce_max(sc1a[:, :], zt, axis=AX.X, negate=True))
                VS(VE.tensor_tensor(sc4a[:, :], zt, sc1a[:, :].broadcast_to([128, 4]), ALU.add))
                VS(VE.tensor_scalar(sc4b[:, :], sc4a[:, :], 0.0, None, ALU.is_ge))
                VS(VE.scalar_tensor_tensor(sc4c[:, :], sc4b[:, :], -1e30, zt,
                                           ALU.mult, ALU.add))
                VS(VE.reduce_max(sc1b[:, :], sc4c[:, :], axis=AX.X))
                _w(SE, sV, sV.c)
                if t == 0:
                    _w(SE, sP, ztp)
                _inc(SE.activation(sc4d[:, :], zt, AF.Exp, bias=sc1a[:, :]), sA, 1)
                _w(VE, sA, sA.c)
                VS(VE.tensor_tensor(sc4a[:, :], zt, sc1b[:, :].broadcast_to([128, 4]), ALU.subtract))
                VS(VE.tensor_scalar(sc4b[:, :], sc4a[:, :], 0.0, None, ALU.is_ge))
                VS(VE.tensor_mul(sc4c[:, :], sc4d[:, :], sc4b[:, :]))
                VS(VE.reduce_sum(sc1c[:, :], sc4c[:, :], axis=AX.X))
                VS(VE.reciprocal(sc1c[:, :], sc1c[:, :]))
                VS(VE.tensor_tensor(gall[:, 4 * t:4 * (t + 1)], sc4c[:, :],
                                    sc1c[:, :].broadcast_to([128, 4]), ALU.mult))
            gv_base = sV.c
            gck('g3')
            gcp = [0] * 8
            for t in range(8):
                _w(TE, sV, gv_base)
                if t >= 2:
                    _w(TE, sV, gcp[t - 2])
                _inc(TE.transpose(psS[0][0:4, 512 * (t % 2):512 * (t % 2) + 128],
                                  gall[:, 4 * t:4 * (t + 1)], idf[:, :]), sP, 1)
                _w(VE, sP, sP.c)
                _inc(VE.tensor_copy(gfm[0:4, 128 * t:128 * (t + 1)],
                                    psS[0][0:4, 512 * (t % 2):512 * (t % 2) + 128]), sV, 1)
                gcp[t] = sV.c
            gck('g4')
            _w(TE, sV, sV.c)
            for j in range(2):
                ins = TE.matmul(psS[0][0:ER, 512 * j:512 * (j + 1)], E4_sb[:, :],
                                gfm[0:4, 512 * j:512 * (j + 1)], start=True, stop=True)
                if j == 1:
                    _inc(ins, sP, 1)
            _w(VE, sP, sP.c)
            for j in range(2):
                ins = VE.tensor_copy(Gsb[:, 512 * j:512 * (j + 1)],
                                     psS[0][0:ER, 512 * j:512 * (j + 1)])
            _inc(ins, sV, 1)
            _w(VE, sV, sV.c)
            for j in range(2):
                ins = VE.tensor_tensor(gmid[:, 512 * j:512 * (j + 1)],
                                       psS[1][0:ER, 512 * j:512 * (j + 1)],
                                       Gsb[:, 512 * j:512 * (j + 1)], ALU.mult)
            _inc(ins, sV, 1)
            return sV.c

        gmid_v = gates_and_gmid(zmid_p, tags=True)
        ckpt('p3')

        def ffn_up_residual(src_dram, dst_dram, gmid_ready, final_out):
            pe_done = [0] * C
            ve_done = [0] * C
            st_mark = [None] * C
            stat_mark2 = [0] * C
            xq_mark = [0] * C
            entry_v, entry_a, entry_p = sV.c, sA.c, sP.c
            entry_t = [sTp[0].c, sTp[1].c]
            stat_p = 0
            for oc in range(C):
                buf = oc % 2
                sx = sXp[buf]
                st = sTp[buf]
                if oc >= 2:
                    _w(SY, sV, ve_done[oc - 2])
                else:
                    _w(SY, sV, entry_v)
                    _w(SY, sA, entry_a)
                _inc(SY.dma_start(out=xst[:, T * buf:T * (buf + 1)],
                                  in_=src_dram[oc, :, :]), sx, 16)
                lm = (sx, sx.c)
                _w(TE, sV, gmid_ready)
                if oc >= 2:
                    _w(TE, sV, ve_done[oc - 2])
                for j in range(2):
                    ins = TE.matmul(pst2(buf, j),
                                    B_sb[:, 128 * oc:128 * (oc + 1)],
                                    gmid[:, 512 * j:512 * (j + 1)], start=True, stop=True)
                    if j == 1:
                        _inc(ins, sP, 1)
                pe_done[oc] = sP.c
                _w(VE, sP, pe_done[oc])
                _wm(VE, lm)
                if oc >= 2:
                    _wm(VE, st_mark[oc - 2])
                else:
                    _w(VE, sTp[buf], entry_t[buf])
                if oc >= 2 and not final_out:
                    _w(VE, sA, xq_mark[oc - 2])
                for j in range(2):
                    ins = VE.tensor_add(rst[:, T * buf + 512 * j:T * buf + 512 * (j + 1)],
                                        pst2(buf, j),
                                        xst[:, T * buf + 512 * j:T * buf + 512 * (j + 1)])
                _inc(ins, sV, 1)
                ve_done[oc] = sV.c
                _w(SY, sV, ve_done[oc])
                _inc(SY.dma_start(out=dst_dram[oc, :, :],
                                  in_=rst[:, T * buf:T * (buf + 1)]), st, 16)
                st_mark[oc] = (st, st.c)
                if not final_out:
                    rv = rst[:, T * buf:T * (buf + 1)]
                    _w(VE, sV, ve_done[oc])
                    _w(VE, sP, stat_mark2[oc - 1] if oc >= 1 else entry_p)
                    _inc(VE.tensor_copy(xbb[:, :], rv), sV, 1)
                    _w(SE, sV, ve_done[oc])
                    _w(SE, sP, stat_mark2[oc - 1] if oc >= 1 else entry_p)
                    _inc(SE.activation(xqq[:, :], rv, AF.Square), sA, 1)
                    xq_mark[oc] = sA.c
                    _w(TE, sV, sV.c)
                    _w(TE, sA, sA.c)
                    emit_stat_mms(oc)
                    stat_mark2[oc] = sP.c
                    if oc == C - 1:
                        stat_p = sP.c
            return stat_p

        stat1_p = ffn_up_residual(x_in, xr1, gmid_v, final_out=False)
        ckpt('p4')
        bc1 = ln_rows_and_bcast(stat1_p)

        h1_v = h_pass(xr1, bc1, with_ffn=False)
        ckpt('p6')

        def proj(w_dram, kind):
            pe_mark = [0] * C
            ev_mark = [(sV, 0)] * C
            st_mark = [None] * C
            vtp_mark = [0] * C
            entry_p = sP.c
            entry_kv = [sKVp[0].c, sKVp[1].c]
            entry_qd = [sQDp[0].c, sQDp[1].c]
            for oc in range(C):
                buf = oc % 2
                sw = sWp[buf]
                h = oc
                if oc >= 2:
                    _w(GP, sP, pe_mark[oc - 2])
                else:
                    _w(GP, sP, entry_p)
                _inc(GP.dma_start(out=Wb[:, D * buf:D * (buf + 1)],
                                  in_=w_dram[oc, :, :]), sw, 16)
                wlm = (sw, sw.c)
                _wm(TE, wlm)
                _w(TE, sV, h1_v)
                if oc >= 2:
                    _wm(TE, ev_mark[oc - 2])
                for kc in range(C):
                    lhs = Wb[:, D * buf + 128 * kc:D * buf + 128 * (kc + 1)]
                    for j in range(2):
                        ins = TE.matmul(pst2(buf, j), lhs,
                                        Hb[:, T * kc + 512 * j:T * kc + 512 * (j + 1)],
                                        start=(kc == 0), stop=(kc == C - 1))
                        if kc == C - 1 and j == 1:
                            _inc(ins, sP, 1)
                pe_mark[oc] = sP.c
                if kind == 'k':
                    kb = oc % 2
                    skv = sKVp[kb]
                    _w(VE, sP, pe_mark[oc])
                    if oc >= 2:
                        _wm(VE, st_mark[oc - 2])
                    else:
                        _w(VE, skv, entry_kv[kb])
                    for j in range(2):
                        ins = VE.tensor_copy(Kst[:, T * kb + 512 * j:T * kb + 512 * (j + 1)],
                                             pst2(buf, j))
                    _inc(ins, sV, 1)
                    ev_mark[oc] = (sV, sV.c)
                    _w(SY, sV, sV.c)
                    _inc(SY.dma_start(
                        out=kv_in.ap()[h * KVH:(h + 1) * KVH].rearrange("(p f) -> p f", p=128),
                        in_=Kst[:, T * kb:T * (kb + 1)]), skv, 16)
                    st_mark[oc] = (skv, skv.c)
                    _w(SE, sV, ev_mark[oc][1])
                    _inc(SE.activation(qsqb[:, :], Kst[:, T * kb:T * (kb + 1)], AF.Square), sA, 1)
                    _w(VE, sA, sA.c)
                    _inc(VE.reduce_max(kmax[:, :], qsqb[:, :], axis=AX.X), sV, 1)
                    _w(TE, sV, sV.c)
                    ins = TE.matmul(pstat[0:1, h:h + 1], onesf[:, :], kmax[:, :],
                                    start=True, stop=True)
                    if oc == C - 1:
                        _inc(ins, sP, 1)
                        _w(VE, sP, sP.c)
                        _inc(VE.tensor_copy(brow[:, :], pstat[0:1, 0:H]), sV, 1)
                        _w(SE, sV, sV.c)
                        _inc(SE.activation(bnd_bf[0:1, 0:H], brow[:, :], AF.Identity), sA, 1)
                        _w(VE, sA, sA.c)
                        _inc(VE.memset(bnd_bf[0:1, H:128], 0.0), sV, 1)
                        _w(SY, sA, sA.c)
                        _w(SY, sV, sV.c)
                        _inc(SY.dma_start(
                            out=kv_in.ap()[2 * H * KVH:2 * H * KVH + 128].rearrange("(p f) -> p f", p=1),
                            in_=bnd_bf[:, :]), sKVp[0], 16)
                elif kind == 'v':
                    vb = oc % 2
                    skv = sKVp[vb]
                    _w(VE, sP, pe_mark[oc])
                    if oc >= 1:
                        _w(VE, sP, vtp_mark[oc - 1])
                    for j in range(2):
                        ins = VE.tensor_copy(Vfm[:, 512 * j:512 * (j + 1)],
                                             pst2(buf, j))
                    _inc(ins, sV, 1)
                    ev_mark[oc] = (sV, sV.c)
                    vfm_v = sV.c
                    for g in range(2):
                        _w(TE, sV, vfm_v)
                        _w(TE, sV, sV.c)
                        for tt in range(4):
                            k = 4 * g + tt
                            ins = TE.transpose(
                                pptb[g % 2][:, 128 * tt:128 * (tt + 1)],
                                Vfm[:, 128 * k:128 * (k + 1)], idb[:, :])
                            if tt == 3:
                                _inc(ins, sP, 1)
                        _w(VE, sP, sP.c)
                        if oc >= 2:
                            _wm(VE, st_mark[oc - 2])
                        else:
                            _w(VE, skv, entry_kv[vb])
                        _inc(VE.tensor_copy(Vtm[:, T * vb + 512 * g:T * vb + 512 * (g + 1)],
                                            pptb[g % 2][:, :]), sV, 1)
                    vtp_mark[oc] = sP.c
                    _w(SY, sV, sV.c)
                    _inc(SY.dma_start(
                        out=kv_in.ap()[(H + h) * KVH:(H + h + 1) * KVH].rearrange("(p f) -> p f", p=128),
                        in_=Vtm[:, T * vb:T * (vb + 1)]), skv, 16)
                    st_mark[oc] = (skv, skv.c)
                else:
                    qb = oc % 2
                    sq = sQDp[qb]
                    _w(SE, sP, pe_mark[oc])
                    if oc >= 2:
                        _wm(SE, st_mark[oc - 2])
                    else:
                        _w(SE, sKVp[qb], sKVp[qb].c)
                        _w(SE, sq, entry_qd[qb])
                    for j in range(2):
                        ins = SE.activation(Qst[:, T * qb + 512 * j:T * qb + 512 * (j + 1)],
                                            pst2(buf, j), AF.Identity, scale=RSQ)
                    _inc(ins, sA, 1)
                    ev_mark[oc] = (sA, sA.c)
                    _w(SY, sA, sA.c)
                    _inc(SY.dma_start(out=qd[h, :, :], in_=Qst[:, T * qb:T * (qb + 1)]), sq, 16)
                    st_mark[oc] = (sq, sq.c)

        proj(wk, 'k')
        ckpt('p7k')
        proj(wv, 'v')
        ckpt('p7v')
        _w(GP, sKVp[0], sKVp[0].c)
        _w(GP, sKVp[1], sKVp[1].c)
        _inc(GP.collective_compute(
            "AllGather", ALU.bypass,
            replica_groups=[[0, 1, 2, 3], [4, 5, 6, 7]],
            ins=[kv_in.ap().opt()], outs=[kv_out.ap().opt()]), sCC, 1)
        ckpt('cc')
        proj(wq, 'q')
        qd_total = [sQDp[0].c, sQDp[1].c]
        ckpt('p7q')

        _w(GP, sCC, 1)
        for r in range(NR):
            _inc(GP.dma_start(
                out=bndall[0:1, :].rearrange("p (h r) -> p h r", r=NR)[:, :, r:r + 1],
                in_=kv_out.ap()[r * KV1 + 2 * H * KVH:r * KV1 + 2 * H * KVH + H]
                    .rearrange("(p h r) -> p h r", p=1, r=1),
                allow_slow_non_contiguous=True), sB, 16)
        _w(VE, sB, sB.c)
        _inc(VE.tensor_reduce(brow[0:1, :].rearrange("p (h r) -> p h r", r=1),
                              bndall[0:1, :].rearrange("p (h r) -> p h r", r=NR),
                              AX.X, ALU.max), sV, 1)
        _w(TE, sV, sV.c)
        _inc(TE.matmul(psml[:, 0:H], onesrow[:, :], brow[:, :], start=True, stop=True), sP, 1)
        _w(VE, sP, sP.c)
        _inc(VE.tensor_copy(Bbc[:, :], psml[:, 0:H]), sV, 1)
        bbc_v = sV.c
        ckpt('p8a')

        hl_mark = [None] * H
        head_pe = [0] * H

        def load_head(h):
            hb = h % 2
            sh = sHp[hb]
            if h >= 2:
                _w(SY, sP, head_pe[h - 2])
            _w(SY, sCC, 1)
            if h < 2:
                _w(SY, sQDp[0], qd_total[0])
                _w(SY, sQDp[1], qd_total[1])
            for r in range(NR):
                _inc(SY.dma_start(
                    out=Kh[:, S * hb + T * r:S * hb + T * (r + 1)],
                    in_=kv_out.ap()[r * KV1 + h * KVH:r * KV1 + (h + 1) * KVH]
                        .rearrange("(p f) -> p f", p=128)), sh, 16)
            for r in range(NR):
                _inc(SY.dma_start(
                    out=Vh[:, S * hb + T * r:S * hb + T * (r + 1)],
                    in_=kv_out.ap()[r * KV1 + (H + h) * KVH:r * KV1 + (H + h + 1) * KVH]
                        .rearrange("(p f) -> p f", p=128)), sh, 16)
            _inc(SY.dma_start(out=Qh[:, T * hb:T * (hb + 1)], in_=qd[h, :, :]), sh, 16)
            hl_mark[h] = (sh, sh.c)

        load_head(0)
        load_head(1)
        exp_hist = [0, 0]
        for h in range(H):
            hb = h % 2
            Khv = Kh[:, S * hb:S * (hb + 1)]
            Vhv = Vh[:, S * hb:S * (hb + 1)]
            Qhv = Qh[:, T * hb:T * (hb + 1)]
            _wm(SE, hl_mark[h])
            _w(SE, sP, sP.c)
            _inc(SE.activation(qsqb[:, :], Qhv, AF.Square), sA, 1)
            _w(TE, sA, sA.c)
            _w(TE, sV, sV.c)
            for qi in range(8):
                ins = TE.matmul(psml[:, 32 + qi:32 + qi + 1],
                                qsqb[:, 128 * qi:128 * (qi + 1)], onesb[:, :],
                                start=True, stop=True)
                if qi == 7:
                    _inc(ins, sP, 1)
            _w(VE, sP, sP.c)
            _w(VE, sV, bbc_v)
            _inc(VE.tensor_tensor(mh1[:, :], psml[:, 32:40],
                                  Bbc[:, h:h + 1].broadcast_to([128, 8]), ALU.mult), sV, 1)
            _w(SE, sV, sV.c)
            _inc(SE.activation(mh1[:, :], mh1[:, :], AF.Sqrt), sA, 1)
            _w(VE, sA, sA.c)
            _inc(VE.tensor_scalar(negmh[:, :], mh1[:, :], -1.0, None, ALU.mult), sV, 1)
            mhat_v = sV.c

            exp_saved = [None] * 8
            t_done = [0] * 8
            lchain_mark = [0] * 8
            for qstep in range(9):
                if qstep < 8:
                    qi = qstep
                    ph = S * (qi % 2)
                    lhs = Qhv[:, 128 * qi:128 * (qi + 1)]
                    exp_done = [0] * 4
                    for qc in range(4):
                        pb = qc % 2
                        _wm(TE, hl_mark[h])
                        _w(TE, sA, exp_hist[0])
                        for j in range(2):
                            ins = TE.matmul(pst2(pb, j), lhs,
                                            Khv[:, 1024 * qc + 512 * j:1024 * qc + 512 * (j + 1)],
                                            start=True, stop=True)
                            if j == 1:
                                _inc(ins, sP, 1)
                        _w(SE, sP, sP.c)
                        if qc == 0:
                            _w(SE, sV, mhat_v)
                            if qi >= 2:
                                _w(SE, sV, lchain_mark[qi - 2])
                                _w(SE, sP, t_done[qi - 2])
                        _inc(SE.activation(Pt[:, ph + 1024 * qc:ph + 1024 * (qc + 1)],
                                           psS[pb][:, :],
                                           AF.Exp, bias=negmh[:, qi:qi + 1],
                                           accum_out=lacc[:, 4 * (qi % 2) + qc:4 * (qi % 2) + qc + 1]), sA, 1)
                        exp_done[qc] = sA.c
                        exp_hist.append(sA.c)
                        exp_hist.pop(0)
                    exp_saved[qi] = exp_done
                if qstep >= 1:
                    qj = qstep - 1
                    phj = S * (qj % 2)
                    exp_done = exp_saved[qj]
                    _w(VE, sA, exp_done[3])
                    _inc(VE.reduce_sum(lsb[:, :], lacc[:, 4 * (qj % 2):4 * (qj % 2) + 4],
                                       axis=AX.X), sV, 1)
                    _w(VE, sV, sV.c)
                    _inc(VE.reciprocal(linv[:, :], lsb[:, :]), sV, 1)
                    linv_v = sV.c
                    lchain_mark[qj] = linv_v
                    _w(TE, sV, linv_v)
                    _inc(TE.transpose(psml[0:1, 384:512], linv[:, :], idf[:, :]), sP, 1)
                    linvT_p = sP.c
                    _w(VE, sP, linvT_p)
                    _inc(VE.tensor_copy(linvrow[:, :], psml[0:1, 384:512]), sV, 1)
                    linvrow_v = sV.c
                    pptev = [0, 0]
                    for g in range(8):
                        gb = g % 2
                        _w(TE, sA, exp_done[g // 2])
                        if g >= 2:
                            _w(TE, sV, pptev[gb])
                        for tt in range(4):
                            k = 4 * g + tt
                            ins = TE.transpose(pptb[gb][:, 128 * tt:128 * (tt + 1)],
                                               Pt[:, phj + 128 * k:phj + 128 * (k + 1)], idb[:, :])
                            if tt == 3:
                                _inc(ins, sP, 1)
                        if g == 3:
                            _w(TE, sV, linvrow_v)
                            _inc(TE.matmul(psml[:, 256:384], onesrow[:, :], linvrow[:, :],
                                           start=True, stop=True), sP, 1)
                            bc_p = sP.c
                        _w(VE, sP, sP.c)
                        _inc(VE.tensor_copy(PT[:, 512 * g:512 * (g + 1)],
                                            pptb[gb][:, :]), sV, 1)
                        pptev[gb] = sV.c
                        if g == 4:
                            _w(VE, sP, bc_p)
                            _inc(VE.tensor_copy(linvbc[:, :], psml[:, 256:384]), sV, 1)
                    t_done[qj] = sP.c
                    lbc_v = sV.c
                    _wm(TE, hl_mark[h])
                    _w(TE, sV, lbc_v)
                    for k in range(C):
                        ins = TE.matmul(psml[:, 128:256], Vhv[:, 128 * k:128 * (k + 1)],
                                        PT[:, 128 * k:128 * (k + 1)],
                                        start=(k == 0), stop=(k == C - 1))
                        if k == C - 1:
                            _inc(ins, sP, 1)
                    _w(VE, sP, sP.c)
                    _w(VE, sV, lbc_v)
                    _inc(VE.tensor_tensor(Hb[:, T * h + 128 * qj:T * h + 128 * (qj + 1)],
                                          psml[:, 128:256], linvbc[:, :], ALU.mult), sV, 1)
            head_pe[h] = sP.c
            if h + 2 < H:
                load_head(h + 2)

        attn_v = sV.c
        attn_a = sA.c
        ckpt('p8')

        pe9 = [0] * C
        ev9 = [0] * C
        st9 = [None] * C
        sm9 = [0] * C
        xq9 = [0] * C
        entry_p9 = sP.c
        entry_v9 = sV.c
        entry_a9 = sA.c
        entry_t9 = [sTp[0].c, sTp[1].c]
        stat2_p = 0
        for oc in range(C):
            buf = oc % 2
            sw = sWp[buf]
            sx = sXp[buf]
            st = sTp[buf]
            if oc >= 2:
                _w(GP, sP, pe9[oc - 2])
            else:
                _w(GP, sP, entry_p9)
            _inc(GP.dma_start(out=Wb[:, D * buf:D * (buf + 1)], in_=wo[oc, :, :]), sw, 16)
            wlm = (sw, sw.c)
            if oc >= 2:
                _w(SY, sV, ev9[oc - 2])
            else:
                _w(SY, sV, entry_v9)
                _w(SY, sA, entry_a9)
            _inc(SY.dma_start(out=xst[:, T * buf:T * (buf + 1)], in_=xr1[oc, :, :]), sx, 16)
            lm = (sx, sx.c)
            _wm(TE, wlm)
            _w(TE, sV, attn_v)
            _w(TE, sA, attn_a)
            if oc >= 2:
                _w(TE, sV, ev9[oc - 2])
            for kc in range(C):
                lhs = Wb[:, D * buf + 128 * kc:D * buf + 128 * (kc + 1)]
                for j in range(2):
                    ins = TE.matmul(pst2(buf, j), lhs,
                                    Hb[:, T * kc + 512 * j:T * kc + 512 * (j + 1)],
                                    start=(kc == 0), stop=(kc == C - 1))
                    if kc == C - 1 and j == 1:
                        _inc(ins, sP, 1)
            pe9[oc] = sP.c
            _w(VE, sP, pe9[oc])
            _wm(VE, lm)
            if oc >= 2:
                _wm(VE, st9[oc - 2])
                _w(VE, sA, xq9[oc - 2])
            else:
                _w(VE, sTp[buf], entry_t9[buf])
            for j in range(2):
                ins = VE.tensor_add(rst[:, T * buf + 512 * j:T * buf + 512 * (j + 1)],
                                    pst2(buf, j),
                                    xst[:, T * buf + 512 * j:T * buf + 512 * (j + 1)])
            _inc(ins, sV, 1)
            ev9[oc] = sV.c
            _w(SY, sV, ev9[oc])
            _inc(SY.dma_start(out=xr2[oc, :, :], in_=rst[:, T * buf:T * (buf + 1)]), st, 16)
            st9[oc] = (st, st.c)
            rv = rst[:, T * buf:T * (buf + 1)]
            _w(VE, sV, ev9[oc])
            _w(VE, sP, sm9[oc - 1] if oc >= 1 else entry_p9)
            _inc(VE.tensor_copy(xbb[:, :], rv), sV, 1)
            _w(SE, sV, ev9[oc])
            _w(SE, sP, sm9[oc - 1] if oc >= 1 else entry_p9)
            _inc(SE.activation(xqq[:, :], rv, AF.Square), sA, 1)
            xq9[oc] = sA.c
            _w(TE, sV, sV.c)
            _w(TE, sA, sA.c)
            emit_stat_mms(oc)
            sm9[oc] = sP.c
            if oc == C - 1:
                stat2_p = sP.c

        ckpt('p9')
        bc2 = ln_rows_and_bcast(stat2_p)

        zmid2_p = h_pass(xr2, bc2, with_ffn=True)
        ckpt('p11')
        gmid2_v = gates_and_gmid(zmid2_p)
        ffn_up_residual(xr2, y_out, gmid2_v, final_out=True)

        for s in [*sTp, *sKVp, *sQDp]:
            SY.wait_ge(s.h, s.c)
        GP.wait_ge(sTp[0].h, sTp[0].c)
        GP.wait_ge(sTp[1].h, sTp[1].c)


_CACHED = {}


def _build_cached():
    if "nc" not in _CACHED:
        _CACHED["nc"] = build()
    return _CACHED["nc"]


def kernel(L0_attn_x, L0_attn_w_q, L0_attn_w_k, L0_attn_w_v, L0_attn_w_o,
           L0_ffn_A, L0_ffn_B, L0_ffn_Wr):
    x = np.asarray(L0_attn_x, dtype=np.float32)
    wqm = np.asarray(L0_attn_w_q, dtype=np.float32)
    wkm = np.asarray(L0_attn_w_k, dtype=np.float32)
    wvm = np.asarray(L0_attn_w_v, dtype=np.float32)
    wom = np.asarray(L0_attn_w_o, dtype=np.float32)
    A = np.asarray(L0_ffn_A, dtype=np.float32)
    B = np.asarray(L0_ffn_B, dtype=np.float32)
    Wr = np.asarray(L0_ffn_Wr, dtype=np.float32)

    def blk(w):
        return np.ascontiguousarray(
            w.T.reshape(C, 128, C, 128).transpose(2, 1, 0, 3).reshape(C, 128, D))

    A_T = A.transpose(2, 0, 1).reshape(D, ER)
    shared = dict(
        wq_blk=blk(wqm), wk_blk=blk(wkm), wv_blk=blk(wvm), wo_blk=blk(wom),
        a_blk=np.ascontiguousarray(A_T.reshape(C, 128, ER).transpose(1, 0, 2).reshape(128, C * ER)),
        b_e=np.ascontiguousarray(B.transpose(0, 2, 1).reshape(ER, D)),
        wr_blk=np.ascontiguousarray(Wr.T.reshape(C, 128, 4).transpose(1, 0, 2).reshape(128, 128)),
        e4=np.kron(np.eye(4, dtype=np.float32), np.ones((1, 16), np.float32)) * 2.0,
        idb=np.eye(128, dtype=BF), idf=np.eye(128, dtype=np.float32),
        onesb=np.ones((128, 1), dtype=BF), onesf=np.ones((128, 1), np.float32),
        onesrow=np.ones((1, 128), np.float32))
    in_maps = []
    for c in range(8):
        b = c // 4
        s0 = T * (c % 4)
        m = dict(shared)
        m["x_fm"] = np.ascontiguousarray(x[b, s0:s0 + T, :].T).reshape(C, 128, T)
        in_maps.append(m)

    nc = _build_cached()
    res = run_bass_kernel_spmd(nc, in_maps, core_ids=list(range(8)))
    out = np.empty((2, S, D), dtype=np.float32)
    for c in range(8):
        b = c // 4
        s0 = T * (c % 4)
        out[b, s0:s0 + T, :] = res.results[c]["y"].reshape(D, T).T
    return out


# version 09 (final)
# speedup vs baseline: 1.0235x; 1.0235x over previous
import contextlib
import numpy as np
import ml_dtypes
import concourse.bass as bass
import concourse.mybir as mybir
from concourse.bass_utils import run_bass_kernel_spmd

F32 = mybir.dt.float32
BF16 = mybir.dt.bfloat16
AF = mybir.ActivationFunctionType
ALU = mybir.AluOpType
AX = mybir.AxisListType
BF = ml_dtypes.bfloat16

D = 4096
T = 1024
C = 32
H = 32
S = 4096
NR = 4
ER = 64
KVH = 131072
KV1 = 2 * H * KVH + 128
RSQ = 0.08838834764831845
EPS = 1e-5


class Sem:
    def __init__(self, es, nc, name):
        self.h = es.enter_context(nc.semaphore(name))
        self.c = 0


def _inc(instr, s, k):
    instr.then_inc(s.h, k)
    s.c += k


def _w(eng, s, v):
    assert v <= s.c, f"wait ahead of emission: {v} > {s.c}"
    if v > 0:
        eng.wait_ge(s.h, v)


def _wm(eng, mark):
    sem, val = mark
    _w(eng, sem, val)


class _Stop(Exception):
    pass


def build(stop=None, dbg=False):
    nc = bass.Bass("TRN2", target_bir_lowering=False, detect_race_conditions=False)
    es = contextlib.ExitStack()
    try:
        _build_body(nc, es, stop, dbg)
    except _Stop:
        pass
    return nc


def _build_body(nc, es, stop, dbg=False):
    x_in = nc.dram_tensor("x_fm", [C, 128, T], F32, kind="ExternalInput")
    wq = nc.dram_tensor("wq_blk", [C, 128, D], F32, kind="ExternalInput")
    wk = nc.dram_tensor("wk_blk", [C, 128, D], F32, kind="ExternalInput")
    wv = nc.dram_tensor("wv_blk", [C, 128, D], F32, kind="ExternalInput")
    wo = nc.dram_tensor("wo_blk", [C, 128, D], F32, kind="ExternalInput")
    a_blk = nc.dram_tensor("a_blk", [128, C * ER], F32, kind="ExternalInput")
    b_e = nc.dram_tensor("b_e", [ER, D], F32, kind="ExternalInput")
    wr_blk = nc.dram_tensor("wr_blk", [128, 128], F32, kind="ExternalInput")
    e4 = nc.dram_tensor("e4", [4, ER], F32, kind="ExternalInput")
    idb_in = nc.dram_tensor("idb", [128, 128], BF16, kind="ExternalInput")
    idf_in = nc.dram_tensor("idf", [128, 128], F32, kind="ExternalInput")
    onesb_in = nc.dram_tensor("onesb", [128, 1], BF16, kind="ExternalInput")
    onesf_in = nc.dram_tensor("onesf", [128, 1], F32, kind="ExternalInput")
    onesrow_in = nc.dram_tensor("onesrow", [1, 128], F32, kind="ExternalInput")

    y_out = nc.dram_tensor("y", [C, 128, T], F32, kind="ExternalOutput")

    _ik = dict(kind="ExternalOutput") if dbg else {}
    xr1 = nc.dram_tensor("xr1", [C, 128, T], F32, **_ik)
    xr2 = nc.dram_tensor("xr2", [C, 128, T], F32, **_ik)
    qd = nc.dram_tensor("q_dram", [H, 128, T], BF16)
    kv_in = nc.dram_tensor("kv_in", [KV1], BF16)
    kv_out = nc.dram_tensor("kv_out", [NR * KV1], BF16)

    with es:
        sb = lambda n, sh, dt: es.enter_context(nc.sbuf_tensor(n, sh, dt))
        psb = lambda n, sh, dt: es.enter_context(nc.psum_tensor(n, sh, dt))

        Hb = sb("Hb", [128, C * T], BF16)
        Wb = sb("Wb", [128, 2 * D], BF16)
        xst = sb("xst", [128, 2 * T], F32)
        rst = sb("rst", [128, 2 * T], F32)
        Mbc = sb("Mbc", [128, T], F32)
        Gsb = Mbc[0:ER, :]
        Rbc = sb("Rbc", [128, T], F32)
        xbb = sb("xbb", [128, T], BF16)
        xqq = sb("xqq", [128, T], BF16)
        Kh = sb("Kh", [128, 2 * S], BF16)
        Vh = sb("Vh", [128, 2 * S], BF16)
        Pt = sb("Pt", [128, 2 * S], BF16)
        PT = sb("PT", [128, S], BF16)
        qsqb = xbb
        qsqrow = sb("qsqrow", [1, T], F32)
        negmh = sb("negmh", [128, 8], F32)
        mh1 = sb("mh1", [128, 8], F32)
        lacc = sb("lacc", [128, 16], F32)
        lsb = sb("lsb", [128, 1], F32)
        linv = sb("linv", [128, 1], F32)
        linvrow = sb("linvrow", [1, 128], F32)
        bndall = linvrow
        linvbc = sb("linvbc", [128, 128], F32)
        brow = sb("brow", [1, H], F32)
        Bbc = sb("Bbc", [128, H], F32)
        Vfm = xqq
        Qst = sb("Qst", [128, 2 * T], BF16)
        Vtm = Qst
        Qh = Qst
        Kst = Pt
        kmax = sb("kmax", [128, 1], F32)
        bnd_bf = sb("bnd_bf", [1, 128], BF16)
        zsb = sb("zsb", [4, T], F32)
        gall = sb("gall", [128, C], F32)
        gfm = sb("gfm", [4, T], BF16)

        gmid = sb("gmid", [ER, T], BF16)
        sc1a = sb("sc1a", [128, 1], F32)
        sc1b = sb("sc1b", [128, 1], F32)
        sc1c = sb("sc1c", [128, 1], F32)
        sc4a = sb("sc4a", [128, 4], F32)
        sc4b = sb("sc4b", [128, 4], F32)
        sc4c = sb("sc4c", [128, 4], F32)
        sc4d = sb("sc4d", [128, 4], F32)
        murow = sb("murow", [1, T], F32)
        rrow = sb("rrow", [1, T], F32)
        rowT2 = sb("rowT2", [1, T], F32)
        rowT1 = qsqrow
        A_sb = sb("A_sb", [128, C * ER], BF16)
        B_sb = sb("B_sb", [ER, D], BF16)
        Wr_sb = sb("Wr_sb", [128, 128], BF16)
        E4_sb = sb("E4_sb", [4, ER], BF16)
        idb = sb("idb_sb", [128, 128], BF16)
        idf = sb("idf_sb", [128, 128], F32)
        onesb = sb("onesb_sb", [128, 1], BF16)
        onesf = sb("onesf_sb", [128, 1], F32)
        onesrow = sb("onesrow_sb", [1, 128], F32)

        pst = [psb(f"ps{i}", [128, 512], F32) for i in range(4)]
        pstat = psb("pstat", [128, T], F32)
        pptb = [psb(f"ppt{i}", [128, 512], BF16) for i in range(2)]
        psml = pstat

        sC = Sem(es, nc, "sC")
        sB = Sem(es, nc, "sB")
        sCC = Sem(es, nc, "sCC")
        sP = Sem(es, nc, "sP")
        sV = Sem(es, nc, "sV")
        sA = Sem(es, nc, "sA")
        sWp = [Sem(es, nc, f"sW{i}") for i in range(2)]
        sXp = [Sem(es, nc, f"sX{i}") for i in range(2)]
        sTp = [Sem(es, nc, f"sT{i}") for i in range(2)]
        sKVp = [Sem(es, nc, f"sKV{i}") for i in range(2)]
        sQDp = [Sem(es, nc, f"sQD{i}") for i in range(2)]
        sHp = [Sem(es, nc, f"sH{i}") for i in range(2)]

        block = es.enter_context(nc.Block())
        TE, VE, SE, SY, GP = nc.tensor, nc.vector, nc.scalar, nc.sync, nc.gpsimd

        def ckpt(tag):
            if stop == tag:
                for s in [*sTp, *sKVp, *sQDp, *sXp, *sHp]:
                    SY.wait_ge(s.h, s.c)
                for s in [*sWp, sC, sB]:
                    GP.wait_ge(s.h, s.c)
                raise _Stop()

        for dst, src in [(idb, idb_in), (idf, idf_in), (onesb, onesb_in),
                         (onesf, onesf_in), (onesrow, onesrow_in),
                         (A_sb, a_blk), (B_sb, b_e), (Wr_sb, wr_blk), (E4_sb, e4)]:
            _inc(GP.dma_start(out=dst[:, :], in_=src[:, :]), sC, 16)
        cst_w = sC.c
        ckpt('p0')

        def emit_stat_mms(c):
            for j in range(2):
                TE.matmul(pstat[0:1, 512 * j:512 * (j + 1)], onesb[:, :],
                          xbb[:, 512 * j:512 * (j + 1)], start=(c == 0), stop=(c == C - 1))
            for j in range(2):
                ins = TE.matmul(pstat[32:33, 512 * j:512 * (j + 1)], onesb[:, :],
                                xqq[:, 512 * j:512 * (j + 1)], start=(c == 0), stop=(c == C - 1))
                if j == 1:
                    _inc(ins, sP, 1)

        def ln_rows_and_bcast(stats_p_inc):
            _w(VE, sP, stats_p_inc)
            _inc(VE.tensor_scalar(murow[:, :], pstat[0:1, 0:T], 1.0 / D, None, ALU.mult), sV, 1)
            _w(VE, sV, sV.c)
            _inc(VE.tensor_mul(rowT1[:, :], murow[:, :], murow[:, :]), sV, 1)
            _w(VE, sV, sV.c)
            _inc(VE.scalar_tensor_tensor(rowT2[:, :], pstat[32:33, 0:T], 1.0 / D,
                                         rowT1[:, :], ALU.mult, ALU.subtract), sV, 1)
            _w(VE, sV, sV.c)
            _inc(VE.tensor_scalar(rowT1[:, :], rowT2[:, :], EPS, None, ALU.add), sV, 1)
            _w(SE, sV, sV.c)
            _inc(SE.activation(rowT2[:, :], rowT1[:, :], AF.Sqrt), sA, 1)
            _w(VE, sA, sA.c)
            _inc(VE.reciprocal(rrow[:, :], rowT2[:, :]), sV, 1)
            _w(TE, sV, sV.c)
            for j in range(2):
                TE.matmul(pst[j][:, :], onesrow[:, :],
                          murow[0:1, 512 * j:512 * (j + 1)], start=True, stop=True)
            for j in range(2):
                ins = TE.matmul(pst[2 + j][:, :], onesrow[:, :],
                                rrow[0:1, 512 * j:512 * (j + 1)], start=True, stop=True)
                if j == 1:
                    _inc(ins, sP, 1)
            _w(VE, sP, sP.c)
            for j in range(2):
                VE.tensor_copy(Mbc[:, 512 * j:512 * (j + 1)], pst[j][:, :])
            for j in range(2):
                ins = VE.tensor_copy(Rbc[:, 512 * j:512 * (j + 1)], pst[2 + j][:, :])
            _inc(ins, sV, 1)
            return sV.c

        ve_mark = [0] * C
        se_mark = [0] * C
        stat_mark = [0] * C
        for c in range(C):
            buf = c % 2
            sx = sXp[buf]
            if c >= 2:
                _w(SY, sV, ve_mark[c - 2])
                _w(SY, sA, se_mark[c - 2])
            _inc(SY.dma_start(out=xst[:, T * buf:T * (buf + 1)],
                              in_=x_in[c, :, :]), sx, 16)
            lm = (sx, sx.c)
            xv = xst[:, T * buf:T * (buf + 1)]
            hv = Hb[:, T * c:T * (c + 1)]
            _wm(VE, lm)
            _inc(VE.tensor_copy(hv, xv), sV, 1)
            ve_mark[c] = sV.c
            _wm(SE, lm)
            _w(SE, sC, cst_w)
            if c >= 1:
                _w(SE, sP, stat_mark[c - 1])
            _inc(SE.activation(xqq[:, :], xv, AF.Square), sA, 1)
            se_mark[c] = sA.c
            _w(TE, sV, ve_mark[c])
            _w(TE, sA, se_mark[c])
            _w(TE, sC, cst_w)
            for j in range(2):
                TE.matmul(pstat[0:1, 512 * j:512 * (j + 1)], onesb[:, :],
                          hv[:, 512 * j:512 * (j + 1)], start=(c == 0), stop=(c == C - 1))
            for j in range(2):
                ins = TE.matmul(pstat[32:33, 512 * j:512 * (j + 1)], onesb[:, :],
                                xqq[:, 512 * j:512 * (j + 1)], start=(c == 0), stop=(c == C - 1))
                if j == 1:
                    _inc(ins, sP, 1)
            stat_mark[c] = sP.c
        ckpt('p1')
        ln_rows_and_bcast(stat_mark[C - 1])
        bc0 = sV.c
        ckpt('p1b')

        def h_pass(src_dram, bc_ready, with_ffn):
            h_done = [0] * C
            entry_v, entry_a, entry_p = sV.c, sA.c, sP.c
            entry_t = [sTp[0].c, sTp[1].c]
            for c in range(C):
                buf = c % 2
                sx = sXp[buf]
                if c >= 2:
                    _w(SY, sV, h_done[c - 2])
                else:
                    _w(SY, sV, entry_v)
                    _w(SY, sA, entry_a)
                _inc(SY.dma_start(out=xst[:, T * buf:T * (buf + 1)],
                                  in_=src_dram[c, :, :]), sx, 16)
                lm = (sx, sx.c)
                xv = xst[:, T * buf:T * (buf + 1)]
                hv = Hb[:, T * c:T * (c + 1)]
                _wm(VE, lm)
                if c == 0:
                    _w(VE, sV, bc_ready)
                    _w(VE, sP, entry_p)
                    _w(VE, sTp[0], entry_t[0])
                    _w(VE, sTp[1], entry_t[1])
                else:
                    _w(VE, sV, h_done[c - 1])
                _inc(VE.tensor_sub(rst[:, 0:T], xv, Mbc[:, :]), sV, 1)
                _w(VE, sV, sV.c)
                _inc(VE.tensor_mul(hv, rst[:, 0:T], Rbc[:, :]), sV, 1)
                h_done[c] = sV.c
                if with_ffn:
                    _w(TE, sV, h_done[c])
                    for j in range(2):
                        TE.matmul(pst[j][0:4, :], Wr_sb[:, 4 * c:4 * (c + 1)],
                                  hv[:, 512 * j:512 * (j + 1)],
                                  start=(c == 0), stop=(c == C - 1))
                    for j in range(2):
                        ins = TE.matmul(pst[2 + j][0:ER, :],
                                        A_sb[:, ER * c:ER * (c + 1)],
                                        hv[:, 512 * j:512 * (j + 1)],
                                        start=(c == 0), stop=(c == C - 1))
                        if c == C - 1 and j == 1:
                            _inc(ins, sP, 1)
            return sP.c if with_ffn else sV.c

        entry_p2 = sP.c
        h_done0 = [0] * C
        for c in range(C):
            hv = Hb[:, T * c:T * (c + 1)]
            if c == 0:
                _w(VE, sV, bc0)
                _w(VE, sP, entry_p2)
            else:
                _w(VE, sV, h_done0[c - 1])
            _inc(VE.tensor_sub(rst[:, 0:T], hv, Mbc[:, :]), sV, 1)
            _w(VE, sV, sV.c)
            _inc(VE.tensor_mul(hv, rst[:, 0:T], Rbc[:, :]), sV, 1)
            h_done0[c] = sV.c
            _w(TE, sV, h_done0[c])
            for j in range(2):
                TE.matmul(pst[j][0:4, :], Wr_sb[:, 4 * c:4 * (c + 1)],
                          hv[:, 512 * j:512 * (j + 1)],
                          start=(c == 0), stop=(c == C - 1))
            for j in range(2):
                ins = TE.matmul(pst[2 + j][0:ER, :],
                                A_sb[:, ER * c:ER * (c + 1)],
                                hv[:, 512 * j:512 * (j + 1)],
                                start=(c == 0), stop=(c == C - 1))
                if c == C - 1 and j == 1:
                    _inc(ins, sP, 1)
        zmid_p = sP.c
        ckpt('p2')

        def gates_and_gmid(zmid_inc, tags=False):
            def gck(t):
                if tags:
                    ckpt(t)
            _w(VE, sP, zmid_inc)
            VE.tensor_copy(zsb[:, 0:512], pst[0][0:4, :])
            _inc(VE.tensor_copy(zsb[:, 512:T], pst[1][0:4, :]), sV, 1)
            zevac = sV.c
            gck('g1')
            _w(TE, sV, zevac)
            for t in range(8):
                ins = TE.transpose(psml[:, 4 * t:4 * (t + 1)],
                                   zsb[0:4, 128 * t:128 * (t + 1)], idf[0:4, 0:4])
                if t == 7:
                    _inc(ins, sP, 1)
            ztp = sP.c
            gck('g2')
            def VS(instr):
                _inc(instr, sV, 1)
                _w(VE, sV, sV.c)
            for t in range(8):
                zt = psml[:, 4 * t:4 * (t + 1)]
                if t == 0:
                    _w(VE, sP, ztp)
                VS(VE.reduce_max(sc1a[:, :], zt, axis=AX.X, negate=True))
                VS(VE.tensor_tensor(sc4a[:, :], zt, sc1a[:, :].broadcast_to([128, 4]), ALU.add))
                VS(VE.tensor_scalar(sc4b[:, :], sc4a[:, :], 0.0, None, ALU.is_ge))
                VS(VE.scalar_tensor_tensor(sc4c[:, :], sc4b[:, :], -1e30, zt,
                                           ALU.mult, ALU.add))
                VS(VE.reduce_max(sc1b[:, :], sc4c[:, :], axis=AX.X))
                _w(SE, sV, sV.c)
                if t == 0:
                    _w(SE, sP, ztp)
                _inc(SE.activation(sc4d[:, :], zt, AF.Exp, bias=sc1a[:, :]), sA, 1)
                _w(VE, sA, sA.c)
                VS(VE.tensor_tensor(sc4a[:, :], zt, sc1b[:, :].broadcast_to([128, 4]), ALU.subtract))
                VS(VE.tensor_scalar(sc4b[:, :], sc4a[:, :], 0.0, None, ALU.is_ge))
                VS(VE.tensor_mul(sc4c[:, :], sc4d[:, :], sc4b[:, :]))
                VS(VE.reduce_sum(sc1c[:, :], sc4c[:, :], axis=AX.X))
                VS(VE.reciprocal(sc1c[:, :], sc1c[:, :]))
                VS(VE.tensor_tensor(gall[:, 4 * t:4 * (t + 1)], sc4c[:, :],
                                    sc1c[:, :].broadcast_to([128, 4]), ALU.mult))
            gv_base = sV.c
            gck('g3')
            gcp = [0] * 8
            for t in range(8):
                _w(TE, sV, gv_base)
                if t >= 2:
                    _w(TE, sV, gcp[t - 2])
                _inc(TE.transpose(pst[t % 2][0:4, 0:128],
                                  gall[:, 4 * t:4 * (t + 1)], idf[:, :]), sP, 1)
                _w(VE, sP, sP.c)
                _inc(VE.tensor_copy(gfm[0:4, 128 * t:128 * (t + 1)],
                                    pst[t % 2][0:4, 0:128]), sV, 1)
                gcp[t] = sV.c
            gck('g4')
            _w(TE, sV, sV.c)
            for j in range(2):
                ins = TE.matmul(pst[j][0:ER, :], E4_sb[:, :],
                                gfm[0:4, 512 * j:512 * (j + 1)], start=True, stop=True)
                if j == 1:
                    _inc(ins, sP, 1)
            _w(VE, sP, sP.c)
            for j in range(2):
                ins = VE.tensor_copy(Gsb[:, 512 * j:512 * (j + 1)], pst[j][0:ER, :])
            _inc(ins, sV, 1)
            _w(VE, sV, sV.c)
            for j in range(2):
                ins = VE.tensor_tensor(gmid[:, 512 * j:512 * (j + 1)],
                                       pst[2 + j][0:ER, :],
                                       Gsb[:, 512 * j:512 * (j + 1)], ALU.mult)
            _inc(ins, sV, 1)
            return sV.c

        gmid_v = gates_and_gmid(zmid_p, tags=True)
        ckpt('p3')

        def ffn_up_residual(src_dram, dst_dram, gmid_ready, final_out):
            pe_done = [0] * C
            ve_done = [0] * C
            st_mark = [None] * C
            stat_mark2 = [0] * C
            xq_mark = [0] * C
            entry_v, entry_a, entry_p = sV.c, sA.c, sP.c
            entry_t = [sTp[0].c, sTp[1].c]
            stat_p = 0
            for oc in range(C):
                buf = oc % 2
                sx = sXp[buf]
                st = sTp[buf]
                if oc >= 2:
                    _w(SY, sV, ve_done[oc - 2])
                else:
                    _w(SY, sV, entry_v)
                    _w(SY, sA, entry_a)
                _inc(SY.dma_start(out=xst[:, T * buf:T * (buf + 1)],
                                  in_=src_dram[oc, :, :]), sx, 16)
                lm = (sx, sx.c)
                _w(TE, sV, gmid_ready)
                if oc >= 2:
                    _w(TE, sV, ve_done[oc - 2])
                for j in range(2):
                    ins = TE.matmul(pst[2 * buf + j][:, :],
                                    B_sb[:, 128 * oc:128 * (oc + 1)],
                                    gmid[:, 512 * j:512 * (j + 1)], start=True, stop=True)
                    if j == 1:
                        _inc(ins, sP, 1)
                pe_done[oc] = sP.c
                _w(VE, sP, pe_done[oc])
                _wm(VE, lm)
                if oc >= 2:
                    _wm(VE, st_mark[oc - 2])
                else:
                    _w(VE, sTp[buf], entry_t[buf])
                if oc >= 2 and not final_out:
                    _w(VE, sA, xq_mark[oc - 2])
                for j in range(2):
                    ins = VE.tensor_add(rst[:, T * buf + 512 * j:T * buf + 512 * (j + 1)],
                                        pst[2 * buf + j][:, :],
                                        xst[:, T * buf + 512 * j:T * buf + 512 * (j + 1)])
                _inc(ins, sV, 1)
                ve_done[oc] = sV.c
                _w(SY, sV, ve_done[oc])
                _inc(SY.dma_start(out=dst_dram[oc, :, :],
                                  in_=rst[:, T * buf:T * (buf + 1)]), st, 16)
                st_mark[oc] = (st, st.c)
                if not final_out:
                    rv = rst[:, T * buf:T * (buf + 1)]
                    _w(VE, sV, ve_done[oc])
                    _w(VE, sP, stat_mark2[oc - 1] if oc >= 1 else entry_p)
                    _inc(VE.tensor_copy(xbb[:, :], rv), sV, 1)
                    _w(SE, sV, ve_done[oc])
                    _w(SE, sP, stat_mark2[oc - 1] if oc >= 1 else entry_p)
                    _inc(SE.activation(xqq[:, :], rv, AF.Square), sA, 1)
                    xq_mark[oc] = sA.c
                    _w(TE, sV, sV.c)
                    _w(TE, sA, sA.c)
                    emit_stat_mms(oc)
                    stat_mark2[oc] = sP.c
                    if oc == C - 1:
                        stat_p = sP.c
            return stat_p

        stat1_p = ffn_up_residual(x_in, xr1, gmid_v, final_out=False)
        ckpt('p4')
        bc1 = ln_rows_and_bcast(stat1_p)

        h1_v = h_pass(xr1, bc1, with_ffn=False)
        ckpt('p6')

        def proj(w_dram, kind):
            pe_mark = [0] * C
            ev_mark = [(sV, 0)] * C
            st_mark = [None] * C
            vtp_mark = [0] * C
            entry_p = sP.c
            entry_kv = [sKVp[0].c, sKVp[1].c]
            entry_qd = [sQDp[0].c, sQDp[1].c]
            for oc in range(C):
                buf = oc % 2
                sw = sWp[buf]
                h = oc
                if oc >= 2:
                    _w(GP, sP, pe_mark[oc - 2])
                else:
                    _w(GP, sP, entry_p)
                _inc(GP.dma_start(out=Wb[:, D * buf:D * (buf + 1)],
                                  in_=w_dram[oc, :, :]), sw, 16)
                wlm = (sw, sw.c)
                _wm(TE, wlm)
                _w(TE, sV, h1_v)
                if oc >= 2:
                    _wm(TE, ev_mark[oc - 2])
                for kc in range(C):
                    lhs = Wb[:, D * buf + 128 * kc:D * buf + 128 * (kc + 1)]
                    for j in range(2):
                        ins = TE.matmul(pst[2 * buf + j][:, :], lhs,
                                        Hb[:, T * kc + 512 * j:T * kc + 512 * (j + 1)],
                                        start=(kc == 0), stop=(kc == C - 1))
                        if kc == C - 1 and j == 1:
                            _inc(ins, sP, 1)
                pe_mark[oc] = sP.c
                if kind == 'k':
                    kb = oc % 2
                    skv = sKVp[kb]
                    _w(VE, sP, pe_mark[oc])
                    if oc >= 2:
                        _wm(VE, st_mark[oc - 2])
                    else:
                        _w(VE, skv, entry_kv[kb])
                    for j in range(2):
                        ins = VE.tensor_copy(Kst[:, T * kb + 512 * j:T * kb + 512 * (j + 1)],
                                             pst[2 * buf + j][:, :])
                    _inc(ins, sV, 1)
                    ev_mark[oc] = (sV, sV.c)
                    _w(SY, sV, sV.c)
                    _inc(SY.dma_start(
                        out=kv_in.ap()[h * KVH:(h + 1) * KVH].rearrange("(p f) -> p f", p=128),
                        in_=Kst[:, T * kb:T * (kb + 1)]), skv, 16)
                    st_mark[oc] = (skv, skv.c)
                    _w(SE, sV, ev_mark[oc][1])
                    _inc(SE.activation(qsqb[:, :], Kst[:, T * kb:T * (kb + 1)], AF.Square), sA, 1)
                    _w(VE, sA, sA.c)
                    _inc(VE.reduce_max(kmax[:, :], qsqb[:, :], axis=AX.X), sV, 1)
                    _w(TE, sV, sV.c)
                    ins = TE.matmul(pstat[0:1, h:h + 1], onesf[:, :], kmax[:, :],
                                    start=True, stop=True)
                    if oc == C - 1:
                        _inc(ins, sP, 1)
                        _w(VE, sP, sP.c)
                        _inc(VE.tensor_copy(brow[:, :], pstat[0:1, 0:H]), sV, 1)
                        _w(SE, sV, sV.c)
                        _inc(SE.activation(bnd_bf[0:1, 0:H], brow[:, :], AF.Identity), sA, 1)
                        _w(VE, sA, sA.c)
                        _inc(VE.memset(bnd_bf[0:1, H:128], 0.0), sV, 1)
                        _w(SY, sA, sA.c)
                        _w(SY, sV, sV.c)
                        _inc(SY.dma_start(
                            out=kv_in.ap()[2 * H * KVH:2 * H * KVH + 128].rearrange("(p f) -> p f", p=1),
                            in_=bnd_bf[:, :]), sKVp[0], 16)
                elif kind == 'v':
                    vb = oc % 2
                    skv = sKVp[vb]
                    _w(VE, sP, pe_mark[oc])
                    if oc >= 1:
                        _w(VE, sP, vtp_mark[oc - 1])
                    for j in range(2):
                        ins = VE.tensor_copy(Vfm[:, 512 * j:512 * (j + 1)],
                                             pst[2 * buf + j][:, :])
                    _inc(ins, sV, 1)
                    ev_mark[oc] = (sV, sV.c)
                    vfm_v = sV.c
                    for g in range(2):
                        _w(TE, sV, vfm_v)
                        _w(TE, sV, sV.c)
                        for tt in range(4):
                            k = 4 * g + tt
                            ins = TE.transpose(
                                pptb[g % 2][:, 128 * tt:128 * (tt + 1)],
                                Vfm[:, 128 * k:128 * (k + 1)], idb[:, :])
                            if tt == 3:
                                _inc(ins, sP, 1)
                        _w(VE, sP, sP.c)
                        if oc >= 2:
                            _wm(VE, st_mark[oc - 2])
                        else:
                            _w(VE, skv, entry_kv[vb])
                        _inc(VE.tensor_copy(Vtm[:, T * vb + 512 * g:T * vb + 512 * (g + 1)],
                                            pptb[g % 2][:, :]), sV, 1)
                    vtp_mark[oc] = sP.c
                    _w(SY, sV, sV.c)
                    _inc(SY.dma_start(
                        out=kv_in.ap()[(H + h) * KVH:(H + h + 1) * KVH].rearrange("(p f) -> p f", p=128),
                        in_=Vtm[:, T * vb:T * (vb + 1)]), skv, 16)
                    st_mark[oc] = (skv, skv.c)
                else:
                    qb = oc % 2
                    sq = sQDp[qb]
                    _w(SE, sP, pe_mark[oc])
                    if oc >= 2:
                        _wm(SE, st_mark[oc - 2])
                    else:
                        _w(SE, sKVp[qb], sKVp[qb].c)
                        _w(SE, sq, entry_qd[qb])
                    for j in range(2):
                        ins = SE.activation(Qst[:, T * qb + 512 * j:T * qb + 512 * (j + 1)],
                                            pst[2 * buf + j][:, :], AF.Identity, scale=RSQ)
                    _inc(ins, sA, 1)
                    ev_mark[oc] = (sA, sA.c)
                    _w(SY, sA, sA.c)
                    _inc(SY.dma_start(out=qd[h, :, :], in_=Qst[:, T * qb:T * (qb + 1)]), sq, 16)
                    st_mark[oc] = (sq, sq.c)

        proj(wk, 'k')
        ckpt('p7k')
        proj(wv, 'v')
        ckpt('p7v')
        _w(GP, sKVp[0], sKVp[0].c)
        _w(GP, sKVp[1], sKVp[1].c)
        _inc(GP.collective_compute(
            "AllGather", ALU.bypass,
            replica_groups=[[0, 1, 2, 3], [4, 5, 6, 7]],
            ins=[kv_in.ap().opt()], outs=[kv_out.ap().opt()]), sCC, 1)
        ckpt('cc')
        proj(wq, 'q')
        qd_total = [sQDp[0].c, sQDp[1].c]
        ckpt('p7q')

        _w(GP, sCC, 1)
        for r in range(NR):
            _inc(GP.dma_start(
                out=bndall[0:1, :].rearrange("p (h r) -> p h r", r=NR)[:, :, r:r + 1],
                in_=kv_out.ap()[r * KV1 + 2 * H * KVH:r * KV1 + 2 * H * KVH + H]
                    .rearrange("(p h r) -> p h r", p=1, r=1),
                allow_slow_non_contiguous=True), sB, 16)
        _w(VE, sB, sB.c)
        _inc(VE.tensor_reduce(brow[0:1, :].rearrange("p (h r) -> p h r", r=1),
                              bndall[0:1, :].rearrange("p (h r) -> p h r", r=NR),
                              AX.X, ALU.max), sV, 1)
        _w(TE, sV, sV.c)
        _inc(TE.matmul(psml[:, 0:H], onesrow[:, :], brow[:, :], start=True, stop=True), sP, 1)
        _w(VE, sP, sP.c)
        _inc(VE.tensor_copy(Bbc[:, :], psml[:, 0:H]), sV, 1)
        bbc_v = sV.c
        ckpt('p8a')

        hl_mark = [None] * H
        head_pe = [0] * H

        def load_head(h):
            hb = h % 2
            sh = sHp[hb]
            if h >= 2:
                _w(SY, sP, head_pe[h - 2])
            _w(SY, sCC, 1)
            if h < 2:
                _w(SY, sQDp[0], qd_total[0])
                _w(SY, sQDp[1], qd_total[1])
            for r in range(NR):
                _inc(SY.dma_start(
                    out=Kh[:, S * hb + T * r:S * hb + T * (r + 1)],
                    in_=kv_out.ap()[r * KV1 + h * KVH:r * KV1 + (h + 1) * KVH]
                        .rearrange("(p f) -> p f", p=128)), sh, 16)
            for r in range(NR):
                _inc(SY.dma_start(
                    out=Vh[:, S * hb + T * r:S * hb + T * (r + 1)],
                    in_=kv_out.ap()[r * KV1 + (H + h) * KVH:r * KV1 + (H + h + 1) * KVH]
                        .rearrange("(p f) -> p f", p=128)), sh, 16)
            _inc(SY.dma_start(out=Qh[:, T * hb:T * (hb + 1)], in_=qd[h, :, :]), sh, 16)
            hl_mark[h] = (sh, sh.c)

        load_head(0)
        load_head(1)
        exp_hist = [0, 0, 0, 0]
        for h in range(H):
            hb = h % 2
            Khv = Kh[:, S * hb:S * (hb + 1)]
            Vhv = Vh[:, S * hb:S * (hb + 1)]
            Qhv = Qh[:, T * hb:T * (hb + 1)]
            _wm(SE, hl_mark[h])
            _w(SE, sP, sP.c)
            _inc(SE.activation(qsqb[:, :], Qhv, AF.Square), sA, 1)
            _w(TE, sA, sA.c)
            _w(TE, sV, sV.c)
            for qi in range(8):
                ins = TE.matmul(psml[:, 32 + qi:32 + qi + 1],
                                qsqb[:, 128 * qi:128 * (qi + 1)], onesb[:, :],
                                start=True, stop=True)
                if qi == 7:
                    _inc(ins, sP, 1)
            _w(VE, sP, sP.c)
            _w(VE, sV, bbc_v)
            _inc(VE.tensor_tensor(mh1[:, :], psml[:, 32:40],
                                  Bbc[:, h:h + 1].broadcast_to([128, 8]), ALU.mult), sV, 1)
            _w(SE, sV, sV.c)
            _inc(SE.activation(mh1[:, :], mh1[:, :], AF.Sqrt), sA, 1)
            _w(VE, sA, sA.c)
            _inc(VE.tensor_scalar(negmh[:, :], mh1[:, :], -1.0, None, ALU.mult), sV, 1)
            mhat_v = sV.c

            exp_saved = [None] * 8
            t_done = [0] * 8
            lchain_mark = [0] * 8
            for qstep in range(9):
                if qstep < 8:
                    qi = qstep
                    ph = S * (qi % 2)
                    lhs = Qhv[:, 128 * qi:128 * (qi + 1)]
                    exp_done = [0] * 8
                    for e8 in range(8):
                        pb = e8 % 4
                        _wm(TE, hl_mark[h])
                        _w(TE, sA, exp_hist[0])
                        _inc(TE.matmul(pst[pb][:, :], lhs,
                                       Khv[:, 512 * e8:512 * (e8 + 1)],
                                       start=True, stop=True), sP, 1)
                        _w(SE, sP, sP.c)
                        if e8 == 0:
                            _w(SE, sV, mhat_v)
                            if qi >= 2:
                                _w(SE, sV, lchain_mark[qi - 2])
                                _w(SE, sP, t_done[qi - 2])
                        _inc(SE.activation(Pt[:, ph + 512 * e8:ph + 512 * (e8 + 1)],
                                           pst[pb][:, :],
                                           AF.Exp, bias=negmh[:, qi:qi + 1],
                                           accum_out=lacc[:, 8 * (qi % 2) + e8:8 * (qi % 2) + e8 + 1]), sA, 1)
                        exp_done[e8] = sA.c
                        exp_hist.append(sA.c)
                        exp_hist.pop(0)
                    exp_saved[qi] = exp_done
                if qstep >= 1:
                    qj = qstep - 1
                    phj = S * (qj % 2)
                    exp_done = exp_saved[qj]
                    _w(VE, sA, exp_done[7])
                    _inc(VE.reduce_sum(lsb[:, :], lacc[:, 8 * (qj % 2):8 * (qj % 2) + 8],
                                       axis=AX.X), sV, 1)
                    _w(VE, sV, sV.c)
                    _inc(VE.reciprocal(linv[:, :], lsb[:, :]), sV, 1)
                    linv_v = sV.c
                    lchain_mark[qj] = linv_v
                    _w(TE, sV, linv_v)
                    _inc(TE.transpose(psml[0:1, 384:512], linv[:, :], idf[:, :]), sP, 1)
                    linvT_p = sP.c
                    _w(VE, sP, linvT_p)
                    _inc(VE.tensor_copy(linvrow[:, :], psml[0:1, 384:512]), sV, 1)
                    linvrow_v = sV.c
                    pptev = [0, 0]
                    for g in range(8):
                        gb = g % 2
                        _w(TE, sA, exp_done[g])
                        if g >= 2:
                            _w(TE, sV, pptev[gb])
                        for tt in range(4):
                            k = 4 * g + tt
                            ins = TE.transpose(pptb[gb][:, 128 * tt:128 * (tt + 1)],
                                               Pt[:, phj + 128 * k:phj + 128 * (k + 1)], idb[:, :])
                            if tt == 3:
                                _inc(ins, sP, 1)
                        if g == 3:
                            _w(TE, sV, linvrow_v)
                            _inc(TE.matmul(psml[:, 256:384], onesrow[:, :], linvrow[:, :],
                                           start=True, stop=True), sP, 1)
                            bc_p = sP.c
                        _w(VE, sP, sP.c)
                        _inc(VE.tensor_copy(PT[:, 512 * g:512 * (g + 1)],
                                            pptb[gb][:, :]), sV, 1)
                        pptev[gb] = sV.c
                        if g == 4:
                            _w(VE, sP, bc_p)
                            _inc(VE.tensor_copy(linvbc[:, :], psml[:, 256:384]), sV, 1)
                    t_done[qj] = sP.c
                    lbc_v = sV.c
                    _wm(TE, hl_mark[h])
                    _w(TE, sV, lbc_v)
                    for k in range(C):
                        ins = TE.matmul(psml[:, 128:256], Vhv[:, 128 * k:128 * (k + 1)],
                                        PT[:, 128 * k:128 * (k + 1)],
                                        start=(k == 0), stop=(k == C - 1))
                        if k == C - 1:
                            _inc(ins, sP, 1)
                    _w(VE, sP, sP.c)
                    _w(VE, sV, lbc_v)
                    _inc(VE.tensor_tensor(Hb[:, T * h + 128 * qj:T * h + 128 * (qj + 1)],
                                          psml[:, 128:256], linvbc[:, :], ALU.mult), sV, 1)
            head_pe[h] = sP.c
            if h + 2 < H:
                load_head(h + 2)

        attn_v = sV.c
        attn_a = sA.c
        ckpt('p8')

        pe9 = [0] * C
        ev9 = [0] * C
        st9 = [None] * C
        sm9 = [0] * C
        xq9 = [0] * C
        entry_p9 = sP.c
        entry_v9 = sV.c
        entry_a9 = sA.c
        entry_t9 = [sTp[0].c, sTp[1].c]
        stat2_p = 0
        for oc in range(C):
            buf = oc % 2
            sw = sWp[buf]
            sx = sXp[buf]
            st = sTp[buf]
            if oc >= 2:
                _w(GP, sP, pe9[oc - 2])
            else:
                _w(GP, sP, entry_p9)
            _inc(GP.dma_start(out=Wb[:, D * buf:D * (buf + 1)], in_=wo[oc, :, :]), sw, 16)
            wlm = (sw, sw.c)
            if oc >= 2:
                _w(SY, sV, ev9[oc - 2])
            else:
                _w(SY, sV, entry_v9)
                _w(SY, sA, entry_a9)
            _inc(SY.dma_start(out=xst[:, T * buf:T * (buf + 1)], in_=xr1[oc, :, :]), sx, 16)
            lm = (sx, sx.c)
            _wm(TE, wlm)
            _w(TE, sV, attn_v)
            _w(TE, sA, attn_a)
            if oc >= 2:
                _w(TE, sV, ev9[oc - 2])
            for kc in range(C):
                lhs = Wb[:, D * buf + 128 * kc:D * buf + 128 * (kc + 1)]
                for j in range(2):
                    ins = TE.matmul(pst[2 * buf + j][:, :], lhs,
                                    Hb[:, T * kc + 512 * j:T * kc + 512 * (j + 1)],
                                    start=(kc == 0), stop=(kc == C - 1))
                    if kc == C - 1 and j == 1:
                        _inc(ins, sP, 1)
            pe9[oc] = sP.c
            _w(VE, sP, pe9[oc])
            _wm(VE, lm)
            if oc >= 2:
                _wm(VE, st9[oc - 2])
                _w(VE, sA, xq9[oc - 2])
            else:
                _w(VE, sTp[buf], entry_t9[buf])
            for j in range(2):
                ins = VE.tensor_add(rst[:, T * buf + 512 * j:T * buf + 512 * (j + 1)],
                                    pst[2 * buf + j][:, :],
                                    xst[:, T * buf + 512 * j:T * buf + 512 * (j + 1)])
            _inc(ins, sV, 1)
            ev9[oc] = sV.c
            _w(SY, sV, ev9[oc])
            _inc(SY.dma_start(out=xr2[oc, :, :], in_=rst[:, T * buf:T * (buf + 1)]), st, 16)
            st9[oc] = (st, st.c)
            rv = rst[:, T * buf:T * (buf + 1)]
            _w(VE, sV, ev9[oc])
            _w(VE, sP, sm9[oc - 1] if oc >= 1 else entry_p9)
            _inc(VE.tensor_copy(xbb[:, :], rv), sV, 1)
            _w(SE, sV, ev9[oc])
            _w(SE, sP, sm9[oc - 1] if oc >= 1 else entry_p9)
            _inc(SE.activation(xqq[:, :], rv, AF.Square), sA, 1)
            xq9[oc] = sA.c
            _w(TE, sV, sV.c)
            _w(TE, sA, sA.c)
            emit_stat_mms(oc)
            sm9[oc] = sP.c
            if oc == C - 1:
                stat2_p = sP.c

        ckpt('p9')
        bc2 = ln_rows_and_bcast(stat2_p)

        zmid2_p = h_pass(xr2, bc2, with_ffn=True)
        ckpt('p11')
        gmid2_v = gates_and_gmid(zmid2_p)
        ffn_up_residual(xr2, y_out, gmid2_v, final_out=True)

        for s in [*sTp, *sKVp, *sQDp]:
            SY.wait_ge(s.h, s.c)
        GP.wait_ge(sTp[0].h, sTp[0].c)
        GP.wait_ge(sTp[1].h, sTp[1].c)


_CACHED = {}


def _build_cached():
    if "nc" not in _CACHED:
        _CACHED["nc"] = build()
    return _CACHED["nc"]


def kernel(L0_attn_x, L0_attn_w_q, L0_attn_w_k, L0_attn_w_v, L0_attn_w_o,
           L0_ffn_A, L0_ffn_B, L0_ffn_Wr):
    x = np.asarray(L0_attn_x, dtype=np.float32)
    wqm = np.asarray(L0_attn_w_q, dtype=np.float32)
    wkm = np.asarray(L0_attn_w_k, dtype=np.float32)
    wvm = np.asarray(L0_attn_w_v, dtype=np.float32)
    wom = np.asarray(L0_attn_w_o, dtype=np.float32)
    A = np.asarray(L0_ffn_A, dtype=np.float32)
    B = np.asarray(L0_ffn_B, dtype=np.float32)
    Wr = np.asarray(L0_ffn_Wr, dtype=np.float32)

    def blk(w):
        return np.ascontiguousarray(
            w.T.reshape(C, 128, C, 128).transpose(2, 1, 0, 3).reshape(C, 128, D))

    A_T = A.transpose(2, 0, 1).reshape(D, ER)
    shared = dict(
        wq_blk=blk(wqm), wk_blk=blk(wkm), wv_blk=blk(wvm), wo_blk=blk(wom),
        a_blk=np.ascontiguousarray(A_T.reshape(C, 128, ER).transpose(1, 0, 2).reshape(128, C * ER)),
        b_e=np.ascontiguousarray(B.transpose(0, 2, 1).reshape(ER, D)),
        wr_blk=np.ascontiguousarray(Wr.T.reshape(C, 128, 4).transpose(1, 0, 2).reshape(128, 128)),
        e4=np.kron(np.eye(4, dtype=np.float32), np.ones((1, 16), np.float32)) * 2.0,
        idb=np.eye(128, dtype=BF), idf=np.eye(128, dtype=np.float32),
        onesb=np.ones((128, 1), dtype=BF), onesf=np.ones((128, 1), np.float32),
        onesrow=np.ones((1, 128), np.float32))
    in_maps = []
    for c in range(8):
        b = c // 4
        s0 = T * (c % 4)
        m = dict(shared)
        m["x_fm"] = np.ascontiguousarray(x[b, s0:s0 + T, :].T).reshape(C, 128, T)
        in_maps.append(m)

    nc = _build_cached()
    res = run_bass_kernel_spmd(nc, in_maps, core_ids=list(range(8)))
    out = np.empty((2, S, D), dtype=np.float32)
    for c in range(8):
        b = c // 4
        s0 = T * (c % 4)
        out[b, s0:s0 + T, :] = res.results[c]["y"].reshape(D, T).T
    return out
